# version 02 (final)
# baseline (speedup 1.0000x reference)
import numpy as np
from contextlib import ExitStack
import concourse.bass as bass
import concourse.mybir as mybir
from concourse.bass_utils import run_bass_kernel_spmd

F32 = mybir.dt.float32
BF16 = mybir.dt.bfloat16
AF = mybir.ActivationFunctionType
ALU = mybir.AluOpType
AXX = mybir.AxisListType.X

P = 128
B, L, D, H, S = 8, 2048, 1024, 2048, 16
E, I, ISH = 8, 2816, 1408
NDC = D // P
NHC = H // P
NIC = I // P
NSC = ISH // P
HALF = L // 2
NS = NHC * S

ENGINES = ("sync", "gpsimd", "vector", "scalar", "tensor")
DEBUG = False


class Prog:
    def __init__(self):
        self.ops = {e: [] for e in ENGINES}
        self.cnt = {}

    def emit(self, eng, fn, waits=(), inc=None):
        post = None
        if inc is not None:
            name, delta = inc if isinstance(inc, tuple) else (inc, 1)
            self.cnt[name] = self.cnt.get(name, 0) + delta
            post = self.cnt[name]
            inc = (name, delta)
        waits = tuple((s2, v) for (s2, v) in waits if v)
        self.ops[eng].append((fn, waits, inc))
        return post

    def replay(self, eng_name, eng, sems):
        seen = {}
        for fn, waits, inc in self.ops[eng_name]:
            for sname, val in waits:
                if seen.get(sname, -1) >= val:
                    continue
                eng.wait_ge(sems[sname], val)
                seen[sname] = val
            instr = fn(eng)
            if inc is not None:
                instr.then_inc(sems[inc[0]], inc[1])


class Unit:
    def __init__(self, ap=None):
        self.ap = ap
        self.w = None
        self.rs = []


def host_prep(inputs):
    g = {}
    g["x_all"] = np.ascontiguousarray(inputs["L0_attn_x"].transpose(0, 2, 1))

    WxT = inputs["L0_attn_W_x"].T
    WgT = inputs["L0_attn_W_g"].T
    WdtT = inputs["L0_attn_W_dt"].T
    WoutT = inputs["L0_attn_W_out"].T

    def slab(WT, nk, nm):
        Kk, Mm = WT.shape
        return np.ascontiguousarray(
            WT.reshape(nk, P, nm, P).transpose(2, 1, 0, 3).reshape(nm, P, nk * P))

    g["wx_t"] = slab(WxT, NDC, NHC)
    g["wg_t"] = slab(WgT, NDC, NHC)
    g["wdt_t"] = slab(WdtT, NDC, NHC)
    g["wout_t"] = slab(WoutT, NHC, NDC)
    g["wgate_t"] = np.ascontiguousarray(inputs["L0_ffn_gate_w"].reshape(NDC, P, E))
    g["wgexp_t"] = np.ascontiguousarray(
        inputs["L0_ffn_Wg"].reshape(E, NDC, P, NIC, P)
        .transpose(0, 3, 2, 1, 4).reshape(E, NIC, P, D))
    g["wuexp_t"] = np.ascontiguousarray(
        inputs["L0_ffn_Wu"].reshape(E, NDC, P, NIC, P)
        .transpose(0, 3, 2, 1, 4).reshape(E, NIC, P, D))
    g["wdexp_t"] = np.ascontiguousarray(
        inputs["L0_ffn_Wd"].reshape(E, NIC, P, NDC, P)
        .transpose(0, 3, 2, 1, 4).reshape(E, NDC, P, I))
    g["wsg_t"] = slab(inputs["L0_ffn_Sg"], NDC, NSC)
    g["wsu_t"] = slab(inputs["L0_ffn_Su"], NDC, NSC)
    g["wsd_t"] = slab(inputs["L0_ffn_Sd"], NSC, NDC)

    def cols(v):
        return np.ascontiguousarray(v.reshape(-1, P).T)

    g["bx_c"] = cols(inputs["L0_attn_b_x"])
    g["bg_c"] = cols(inputs["L0_attn_b_g"])
    g["bdt_c"] = cols(inputs["L0_attn_b_dt"])
    g["cb_c"] = cols(inputs["L0_attn_conv_b"])
    g["bout_c"] = cols(inputs["L0_attn_b_out"])
    g["lng_c"] = cols(inputs["L0_attn_ln_g"])
    g["lnb_c"] = cols(inputs["L0_attn_ln_b"])
    cw = inputs["L0_attn_conv_w"][:, 0, :]
    g["cw_c"] = np.ascontiguousarray(
        cw.reshape(NHC, P, 3).transpose(1, 0, 2).reshape(P, NHC * 3))

    def hs_cols(v):
        return np.ascontiguousarray(
            v.reshape(NHC, P, S).transpose(1, 0, 2).reshape(P, NS))

    g["araw_c"] = hs_cols(inputs["L0_attn_A_raw"])
    g["bssm_c"] = hs_cols(inputs["L0_attn_B_ssm"])
    g["cssm_c"] = hs_cols(inputs["L0_attn_C_ssm"])

    sel = np.zeros((2, P, P), np.float32)
    sel[0, 0, :] = 1.0
    sel[1, 32, :] = 1.0
    g["sel2"] = sel
    sel8 = np.zeros((E, P, P), np.float32)
    for e2 in range(E):
        sel8[e2, e2, :] = 1.0
    g["sel8"] = sel8
    g["ident"] = np.eye(P, dtype=np.float32)
    return g


def build_nc(debug=False, sdump=False):
    nc = bass.Bass("TRN2", target_bir_lowering=False)
    pr = Prog()

    dr = {}
    def din(name, shape):
        dr[name] = nc.dram_tensor(name, list(shape), F32, kind="ExternalInput").ap()
    din("x_all", (B, D, L))
    din("x_own", (D, L))
    din("wx_t", (NHC, P, D)); din("wg_t", (NHC, P, D)); din("wdt_t", (NHC, P, D))
    din("wout_t", (NDC, P, H))
    din("wgate_t", (NDC, P, E))
    din("wgexp_t", (E, NIC, P, D)); din("wuexp_t", (E, NIC, P, D))
    din("wdexp_t", (E, NDC, P, I))
    din("wsg_t", (NSC, P, D)); din("wsu_t", (NSC, P, D)); din("wsd_t", (NDC, P, ISH))
    din("bx_c", (P, NHC)); din("bg_c", (P, NHC)); din("bdt_c", (P, NHC))
    din("cb_c", (P, NHC)); din("cw_c", (P, NHC * 3))
    din("bout_c", (P, NDC)); din("lng_c", (P, NDC)); din("lnb_c", (P, NDC))
    din("araw_c", (P, NS)); din("bssm_c", (P, NS)); din("cssm_c", (P, NS))
    din("sel2", (2, P, P)); din("sel8", (E, P, P)); din("ident", (P, P))
    y_out = nc.dram_tensor("y_fm", [D, L], F32, kind="ExternalOutput").ap()
    dbg = {}
    if sdump and not debug:
        dbg["dbg_sfin"] = nc.dram_tensor("dbg_sfin", [P, NDC * L], F32,
                                         kind="ExternalOutput").ap()
    if debug:
        for name, shape in [("dbg_dt", (P, NHC)), ("dbg_ad", (P, NS)),
                            ("dbg_bf", (P, NS)), ("dbg_xn", (P, NDC * L)),
                            ("dbg_ug", (P, L)), ("dbg_y", (P, L)),
                            ("dbg_s", (P, L)), ("dbg_comb", (P, L)),
                            ("dbg_acc", (P, L)), ("dbg_xb", (P, L)),
                            ("dbg_m", (1, L)), ("dbg_abc", (P, L)),
                            ("dbg_bbc", (P, L)), ("dbg_sums", (1, L)), ("dbg_sp", (P, L)), ("dbg_acol", (P, 1)), ("dbg_dts", (P, NHC)), ("dbg_sfull", (P, L)), ("dbg_m2", (1, L)), ("dbg_r2", (1, L)), ("dbg_lgt", (P, P)), ("dbg_m12", (P, 64)), ("dbg_xo", (P, L)), ("dbg_ao", (P, L)), ("dbg_ybf", (P, NHC * L)), ("dbg_oh", (P, 64)), ("dbg_act", (P, HALF)), ("dbg_accf", (P, NDC * HALF)), ("dbg_sfin", (P, NDC * L)), ("dbg_sfull2", (P, NDC * L)), ("dbg_xo7", (P, L)), ("dbg_rbc", (P, L)), ("dbg_mbc", (P, L)), ("dbg_gsb", (P, HALF)), ("dbg_usb", (P, HALF))]:
            dbg[name] = nc.dram_tensor(name, list(shape), F32,
                                       kind="ExternalOutput").ap()

    ctx = ExitStack()
    def sb(name, shape, dtype):
        return ctx.enter_context(nc.sbuf_tensor(name, list(shape), dtype))

    xnb = sb("xnb", (P, NDC * L), BF16)
    sbf = sb("sbf", (P, NDC * L), BF16)
    rows_own = sb("rows_own", (1, 2 * L), BF16)
    bx_c = sb("bx_c_sb", (P, NHC), F32)
    bg_c = sb("bg_c_sb", (P, NHC), F32)
    bdtn_c = sb("bdtn_c", (P, NHC), F32)
    cb_c = sb("cb_c_sb", (P, NHC), F32)
    cw_c = sb("cw_c_sb", (P, NHC * 3), F32)
    bout_c = sb("bout_c_sb", (P, NDC), F32)
    lng_c = sb("lng_c_sb", (P, NDC), F32)
    lnb_c = sb("lnb_c_sb", (P, NDC), F32)
    ad_c = sb("ad_c", (P, NS), F32)
    bf_c = sb("bf_c", (P, NS), F32)
    dt_c = sb("dt_c", (P, NHC), F32)
    dts_c = sb("dts_c", (P, NHC), F32)
    selb = sb("selb", (P, 2 * P), BF16)
    idb = sb("idb", (P, P), BF16)
    gwb = sb("gwb", (P, NDC * E), BF16)
    onescol = sb("onescol", (P, 1), BF16)

    psA = ctx.enter_context(nc.psum_tensor("psA", [P, L], F32))
    psB = ctx.enter_context(nc.psum_tensor("psB", [P, L], F32))
    uA = Unit(psA)
    uB = Unit(psB)

    def unit_write(unit, fns, waits=()):
        w = list(waits) + list(unit.rs)
        last = None
        for i2, fn in enumerate(fns):
            last = pr.emit("tensor", fn, waits=w if i2 == 0 else (),
                           inc="t" if i2 == len(fns) - 1 else None)
        unit.w = ("t", last)
        unit.rs = []
        return last

    def unit_read(eng, unit, fn, extra_waits=()):
        sem = "v" if eng == "vector" else "s"
        post = pr.emit(eng, fn, waits=[unit.w] + list(extra_waits), inc=sem)
        unit.rs.append((sem, post))
        return post

    def dma_s(fn, waits=()):
        return pr.emit("sync", fn, waits=waits, inc=("qs", 16))

    def dma_g(fn, waits=()):
        return pr.emit("gpsimd", fn, waits=waits, inc=("qg", 16))

    def bcast_rows(unit, sel_slice, src_tile, waits=()):
        fns = [lambda e, ts=ts: e.matmul(
            unit.ap[:, ts * 512:(ts + 1) * 512], sel_slice,
            src_tile[:, ts * 512:(ts + 1) * 512], start=True, stop=True)
            for ts in range(4)]
        return unit_write(unit, fns, waits=waits)

    for name, dst in [("bx_c", bx_c), ("bg_c", bg_c), ("cb_c", cb_c),
                      ("cw_c", cw_c), ("bout_c", bout_c), ("lng_c", lng_c),
                      ("lnb_c", lnb_c)]:
        dma_s(lambda e, d=dst, s2=dr[name]: e.dma_start(out=d[:, :], in_=s2))
    ld_bdt = dma_s(lambda e: e.dma_start(out=dts_c[:, :], in_=dr["bdt_c"]))
    neg_bdt = pr.emit("vector", lambda e: e.tensor_scalar_mul(
        bdtn_c[:, :], dts_c[:, :], -1.0), waits=[("qs", ld_bdt)], inc="v")
    ms_dts = pr.emit("vector", lambda e: e.memset(dts_c[:, :], 0.0),
                     waits=[("v", neg_bdt)], inc="v")
    for n2 in range(2):
        ld_sel = dma_g(lambda e, n2=n2: e.dma_start(
            out=selb[:, n2 * P:(n2 + 1) * P], in_=dr["sel2"][n2]))
    ld_id = dma_g(lambda e: e.dma_start(out=idb[:, :], in_=dr["ident"]))
    for n2 in range(NDC):
        ld_gw = dma_g(lambda e, n2=n2: e.dma_start(
            out=gwb[:, n2 * E:(n2 + 1) * E], in_=dr["wgate_t"][n2]))
    ms_ones = pr.emit("gpsimd", lambda e: e.memset(onescol[:, :], 1.0), inc="g")

    p1 = ExitStack()
    def sb1(name, shape, dtype):
        return p1.enter_context(nc.sbuf_tensor(name, list(shape), dtype))
    xb0 = sb1("xb0", (P, NDC * L), BF16)
    xb1 = sb1("xb1", (P, NDC * L), BF16)
    wdt0 = sb1("wdt0", (P, D), BF16)
    wdt1 = sb1("wdt1", (P, D), BF16)
    sq_b = sb1("sq_b", (P, L), BF16)
    sp_f = sb1("sp_f", (P, L), F32)
    acol = sb1("acol", (P, 1), F32)
    stageA_b = sb1("stageA_b", (P, L), BF16)
    stageB_b = sb1("stageB_b", (P, L), BF16)
    abc_b = sb1("abc_b", (P, L), BF16)
    bbc_b = sb1("bbc_b", (P, L), BF16)
    srow_m = sb1("srow_m", (1, L), F32)
    srow_t = sb1("srow_t", (1, L), F32)
    araw_c = sb1("araw_c_sb", (P, NS), F32)
    bssm_c = sb1("bssm_c_sb", (P, NS), F32)
    cssm_c = sb1("cssm_c_sb", (P, NS), F32)

    for name, dst in [("araw_c", araw_c), ("bssm_c", bssm_c),
                      ("cssm_c", cssm_c)]:
        dma_s(lambda e, d=dst, s2=dr[name]: e.dma_start(out=d[:, :], in_=s2))
    consts_qs = pr.cnt.get("qs", 0)
    ms_rowp = pr.emit("gpsimd", lambda e: e.memset(stageA_b[:, :], 0.0), inc="g")
    ms_rowp = pr.emit("gpsimd", lambda e: e.memset(stageB_b[:, :], 0.0), inc="g")

    xb_units = [Unit(xb0), Unit(xb1)]
    wdt_units = [Unit(wdt0), Unit(wdt1)]
    u_sq = Unit(sq_b)
    u_spf = Unit(sp_f)

    def load_xb(bi, buf_i):
        u = xb_units[buf_i]
        last = None
        for dc in range(NDC):
            last = dma_g(lambda e, dc=dc, bi=bi, bb=u.ap: e.dma_start(
                out=bb[:, dc * L:(dc + 1) * L],
                in_=dr["x_all"][bi, dc * P:(dc + 1) * P, :]),
                waits=list(u.rs) if dc == 0 else ())
        u.rs = []
        u.w = ("qg", last)

    def load_wdt(hc, buf_i):
        u = wdt_units[buf_i]
        d2 = dma_g(lambda e, hc=hc, bb=u.ap: e.dma_start(
            out=bb[:, :], in_=dr["wdt_t"][hc]), waits=list(u.rs))
        u.rs = []
        u.w = ("qg", d2)

    load_xb(0, 0)
    load_wdt(0, 0)
    load_wdt(1, 1)

    def ln_rows(r0, rt, u0, u1, want_b=True):
        pa = u0.ap[0:1, :]
        pb = u1.ap[0:1, :]
        pr.emit("vector", lambda e: e.tensor_scalar_mul(
            r0, pa, 1.0 / D), waits=[u0.w, u1.w], inc="v")
        pr.emit("vector", lambda e: e.tensor_scalar_mul(
            pb, pb, 1.0 / D), inc="v")
        pr.emit("vector", lambda e: e.tensor_tensor(
            rt, r0, r0, ALU.mult), inc="v")
        pr.emit("vector", lambda e: e.tensor_sub(pb, pb, rt), inc="v")
        v3 = pr.emit("vector", lambda e: e.tensor_scalar_add(
            pb, pb, 1e-5), inc="v")
        s1 = pr.emit("scalar", lambda e: e.activation(
            pb, pb, AF.Sqrt, bias=0.0, scale=1.0),
            waits=[("v", v3)], inc="s")
        v6 = pr.emit("vector", lambda e: e.reciprocal(pa, pb),
                     waits=[("s", s1)], inc="v")
        if want_b:
            pr.emit("vector", lambda e: e.tensor_tensor(
                r0, r0, pa, ALU.mult), inc="v")
            v6 = pr.emit("vector", lambda e: e.tensor_scalar_mul(
                r0, r0, -1.0), inc="v")
        u0.rs.append(("v", v6))
        u1.rs.append(("v", v6))
        return v6

    last_dt_mm = None
    for bi in range(B):
        xb_u = xb_units[bi % 2]
        xb = xb_u.ap
        if bi + 1 < B:
            load_xb(bi + 1, (bi + 1) % 2)

        fns = []
        for ts in range(4):
            for dc in range(NDC):
                fns.append(lambda e, ts=ts, dc=dc, xb=xb: e.matmul(
                    psA[0:1, ts * 512:(ts + 1) * 512], onescol[:, :],
                    xb[:, dc * L + ts * 512: dc * L + (ts + 1) * 512],
                    start=(dc == 0), stop=(dc == NDC - 1)))
        unit_write(uA, fns, waits=[xb_u.w, ("g", ms_ones)])
        last_mm = uA.w[1]
        first_sq = True
        for dc in range(NDC):
            sqd = pr.emit("scalar", lambda e, dc=dc, xb=xb: e.activation(
                sq_b[:, :], xb[:, dc * L:(dc + 1) * L], AF.Square,
                bias=0.0, scale=1.0), waits=[xb_u.w] + list(u_sq.rs), inc="s")
            u_sq.rs = []
            for ts in range(4):
                w2 = []
                if ts == 0:
                    w2.append(("s", sqd))
                    if first_sq:
                        w2.extend(uB.rs)
                        uB.rs = []
                        first_sq = False
                last_mm = pr.emit(
                    "tensor", lambda e, ts=ts, dc=dc: e.matmul(
                        psB[0:1, ts * 512:(ts + 1) * 512], onescol[:, :],
                        sq_b[:, ts * 512:(ts + 1) * 512],
                        start=(dc == 0), stop=(dc == NDC - 1)),
                    waits=w2, inc="t" if ts == 3 else None)
            u_sq.rs.append(("t", last_mm))
        uB.w = ("t", last_mm)
        uB.rs = []

        if debug and bi == 0:
            vps = unit_read("vector", uA, lambda e: e.tensor_copy(
                srow_t[0:1, :], psA[0:1, :]))
            dma_s(lambda e: e.dma_start(out=dbg["dbg_sums"],
                                        in_=srow_t[0:1, :]),
                  waits=[("v", vps)])
            dma_g(lambda e: e.dma_start(out=dbg["dbg_xb"], in_=xb[:, 0:L]),
                  waits=[xb_u.w])
        v6 = ln_rows(srow_m[0:1, :], srow_t[0:1, :], uA, uB)
        if bi == 0:
            pr.emit("vector", lambda e: e.tensor_copy(
                rows_own[0:1, 0:L], psA[0:1, :]), inc="v")
            pr.emit("vector", lambda e: e.tensor_copy(
                rows_own[0:1, L:2 * L], srow_m[0:1, :]), inc="v")
        pr.emit("vector", lambda e: e.tensor_copy(
            stageA_b[0:1, :], psA[0:1, :]),
            waits=[("g", ms_rowp)], inc="v")
        v8 = pr.emit("vector", lambda e: e.tensor_copy(
            stageB_b[0:1, :], srow_m[0:1, :]), inc="v")
        bcast_rows(uA, selb[:, 0:P], stageA_b,
                   waits=[("v", v8), ("qg", ld_sel)])
        bcast_rows(uB, selb[:, 0:P], stageB_b, waits=[("v", v8)])
        ea = unit_read("scalar", uA, lambda e: e.activation(
            abc_b[:, :], psA[:, :], AF.Copy, bias=0.0, scale=1.0))
        eb = unit_read("scalar", uB, lambda e: e.activation(
            bbc_b[:, :], psB[:, :], AF.Copy, bias=0.0, scale=1.0))
        xn_dst = xnb if bi == 0 else xb
        if debug and bi == 0:
            dma_s(lambda e: e.dma_start(out=dbg["dbg_m"], in_=srow_m[0:1, :]),
                  waits=[("v", v6)])
            dma_g(lambda e: e.dma_start(out=dbg["dbg_abc"], in_=abc_b[:, :]),
                  waits=[("s", ea)])
            dma_g(lambda e: e.dma_start(out=dbg["dbg_bbc"], in_=bbc_b[:, :]),
                  waits=[("s", eb)])
        vd = None
        for dc in range(NDC):
            pr.emit("vector", lambda e, dc=dc, xb=xb: e.tensor_tensor(
                xb[:, dc * L:(dc + 1) * L], xb[:, dc * L:(dc + 1) * L],
                abc_b[:, :], ALU.mult), waits=[("s", ea), ("s", eb)], inc="v")
            vd = pr.emit("vector", lambda e, dc=dc, xb=xb, xd=xn_dst:
                         e.tensor_tensor(
                             xd[:, dc * L:(dc + 1) * L],
                             xb[:, dc * L:(dc + 1) * L],
                             bbc_b[:, :], ALU.add), inc="v")
        xn_src = xn_dst
        if debug and bi == 0:
            dma_g(lambda e: e.dma_start(out=dbg["dbg_xn"], in_=xnb[:, :]),
                  waits=[("v", vd)])

        for hc in range(NHC):
            unit = uA if hc % 2 == 0 else uB
            wdt_u = wdt_units[hc % 2]
            fns = []
            for ts in range(4):
                for dc in range(NDC):
                    fns.append(lambda e, ts=ts, dc=dc, u2=unit, xs=xn_src,
                               wb=wdt_u.ap: e.matmul(
                                   u2.ap[:, ts * 512:(ts + 1) * 512],
                                   wb[:, dc * P:(dc + 1) * P],
                                   xs[:, dc * L + ts * 512:
                                      dc * L + (ts + 1) * 512],
                                   start=(dc == 0), stop=(dc == NDC - 1)))
            last_dt_mm = unit_write(unit, fns, waits=[("v", vd), wdt_u.w])
            wdt_u.rs.append(("t", last_dt_mm))
            if not (bi == B - 1 and hc >= NHC - 2):
                load_wdt((hc + 2) % NHC, hc % 2)
            unit_read("scalar", unit, lambda e, hc=hc, u2=unit: e.activation(
                sp_f[:, :], u2.ap[:, :], AF.Sigmoid,
                bias=bdtn_c[:, hc:hc + 1], scale=-1.0),
                extra_waits=[("qs", consts_qs), ("v", neg_bdt)]
                + list(u_spf.rs))
            u_spf.rs = []
            ln = pr.emit("scalar", lambda e: e.activation(
                sp_f[:, :], sp_f[:, :], AF.Ln, bias=0.0, scale=1.0,
                accum_out=acol[:, :]), inc="s")
            if debug and bi == 0 and hc == 0:
                dma_s(lambda e: e.dma_start(out=dbg["dbg_sp"], in_=sp_f[:, :]),
                      waits=[("s", ln)])
                dma_s(lambda e: e.dma_start(out=dbg["dbg_acol"], in_=acol[:, :]),
                      waits=[("s", ln)])
            va = pr.emit("vector", lambda e, hc=hc: e.tensor_tensor(
                dts_c[:, hc:hc + 1], dts_c[:, hc:hc + 1], acol[:, :],
                ALU.add), waits=[("s", ln), ("v", ms_dts)], inc="v")
            u_spf.rs.append(("v", va))
        xb_u.rs.append(("t", last_dt_mm))
        xb_u.rs.append(("v", vd))

    if debug:
        dma_s(lambda e: e.dma_start(out=dbg["dbg_dts"], in_=dts_c[:, :]),
              waits=[("v", pr.cnt["v"])])
    pr.emit("vector", lambda e: e.tensor_scalar_mul(
        dt_c[:, :], dts_c[:, :], -1.0 / (B * L)), inc="v")
    pr.emit("vector", lambda e: e.tensor_scalar_max(
        dt_c[:, :], dt_c[:, :], 0.001), inc="v")
    vdt = pr.emit("vector", lambda e: e.tensor_scalar_min(
        dt_c[:, :], dt_c[:, :], 2.0), inc="v")
    sA = pr.emit("scalar", lambda e: e.activation(
        ad_c[:, :], araw_c[:, :], AF.Sigmoid, bias=0.0, scale=-1.0),
        waits=[("qs", consts_qs)], inc="s")
    sA = pr.emit("scalar", lambda e: e.activation(
        ad_c[:, :], ad_c[:, :], AF.Ln, bias=0.0, scale=1.0), inc="s")
    lastv = None
    for hc in range(NHC):
        lastv = pr.emit("vector", lambda e, hc=hc: e.tensor_scalar(
            bf_c[:, hc * S:(hc + 1) * S], ad_c[:, hc * S:(hc + 1) * S],
            dt_c[:, hc:hc + 1], None, ALU.mult),
            waits=[("v", vdt), ("s", sA)], inc="v")
    vkeepA = pr.emit("vector", lambda e: e.tensor_copy(
        sp_f[:, 0:NS], ad_c[:, :]), waits=[("v", lastv)] + u_spf.rs, inc="v")
    sE = pr.emit("scalar", lambda e: e.activation(
        ad_c[:, :], bf_c[:, :], AF.Exp, bias=0.0, scale=1.0),
        waits=[("v", vkeepA)], inc="s")
    pr.emit("vector", lambda e: e.tensor_scalar_add(
        sp_f[:, 0:NS], sp_f[:, 0:NS], 1e-6),
        waits=[("v", vkeepA), ("s", sE)], inc="v")
    pr.emit("vector", lambda e: e.reciprocal(
        sp_f[:, NS:2 * NS], sp_f[:, 0:NS]), inc="v")
    pr.emit("vector", lambda e: e.tensor_scalar_add(
        bf_c[:, :], ad_c[:, :], -1.0), inc="v")
    pr.emit("vector", lambda e: e.tensor_tensor(
        bf_c[:, :], bf_c[:, :], sp_f[:, NS:2 * NS], ALU.mult), inc="v")
    pr.emit("vector", lambda e: e.tensor_tensor(
        bf_c[:, :], bf_c[:, :], bssm_c[:, :], ALU.mult), inc="v")
    precomp_done = pr.emit("vector", lambda e: e.tensor_tensor(
        bf_c[:, :], bf_c[:, :], cssm_c[:, :], ALU.mult), inc="v")
    if debug:
        dma_s(lambda e: e.dma_start(out=dbg["dbg_dt"], in_=dt_c[:, :]),
              waits=[("v", precomp_done)])
        dma_s(lambda e: e.dma_start(out=dbg["dbg_ad"], in_=ad_c[:, :]))
        dma_s(lambda e: e.dma_start(out=dbg["dbg_bf"], in_=bf_c[:, :]))
    p1.close()

    def eng_bar():
        return [("v", pr.cnt.get("v", 0)), ("s", pr.cnt.get("s", 0)),
                ("t", pr.cnt.get("t", 0)), ("qg", pr.cnt.get("qg", 0)),
                ("qs", pr.cnt.get("qs", 0))]

    bar_p3 = eng_bar()
    p2 = ExitStack()
    def sb2(name, shape, dtype):
        return p2.enter_context(nc.sbuf_tensor(name, list(shape), dtype))
    ybf = sb2("ybf", (P, NHC * L), BF16)

    p2a = ExitStack()
    def sb2a(name, shape, dtype):
        return p2a.enter_context(nc.sbuf_tensor(name, list(shape), dtype))
    u_f = sb2a("u_f", (P, L), F32)
    g_f = sb2a("g_f", (P, L), F32)
    h_f = sb2a("h_f", (P, L), F32)
    st_f = sb2a("st_f", (P, L), F32)
    ya_f = sb2a("ya_f", (P, L), F32)
    wxg0 = sb2a("wxg0", (P, 2 * D), BF16)
    wxg1 = sb2a("wxg1", (P, 2 * D), BF16)

    wxg_units = [Unit(wxg0), Unit(wxg1)]
    u_unit = Unit(u_f)
    g_unit = Unit(g_f)
    h_unit = Unit(h_f)

    def load_wxg(hc, buf_i, bar=()):
        u = wxg_units[buf_i]
        dma_g(lambda e, hc=hc, b2=u.ap: e.dma_start(
            out=b2[:, 0:D], in_=dr["wx_t"][hc]),
            waits=list(u.rs) + list(bar))
        d2 = dma_g(lambda e, hc=hc, b2=u.ap: e.dma_start(
            out=b2[:, D:2 * D], in_=dr["wg_t"][hc]))
        u.rs = []
        u.w = ("qg", d2)

    load_wxg(0, 0, bar=bar_p3)
    load_wxg(1, 1)

    vy = None
    for hc in range(NHC):
        wxg_u = wxg_units[hc % 2]
        wxg = wxg_u.ap

        def mk_mm(ps, woff, ts, dc, wxg=wxg):
            return lambda e: e.matmul(
                ps[:, ts * 512:(ts + 1) * 512],
                wxg[:, woff + dc * P: woff + (dc + 1) * P],
                xnb[:, dc * L + ts * 512: dc * L + (ts + 1) * 512],
                start=(dc == 0), stop=(dc == NDC - 1))

        fns = [mk_mm(psA, 0, ts, dc) for ts in range(4) for dc in range(NDC)]
        tw = unit_write(uA, fns, waits=[wxg_u.w, ("v", precomp_done)])
        wxg_u.rs.append(("t", tw))
        fns = [mk_mm(psB, D, ts, dc) for ts in range(4) for dc in range(NDC)]
        tw = unit_write(uB, fns, waits=[wxg_u.w])
        wxg_u.rs.append(("t", tw))
        if hc + 2 < NHC:
            load_wxg(hc + 2, hc % 2)

        ue = unit_read("scalar", uA, lambda e, hc=hc: e.activation(
            u_f[:, :], psA[:, :], AF.Identity, bias=bx_c[:, hc:hc + 1],
            scale=1.0), extra_waits=list(u_unit.rs))
        u_unit.rs = []
        u_unit.w = ("s", ue)
        ge = unit_read("scalar", uB, lambda e, hc=hc: e.activation(
            g_f[:, :], psB[:, :], AF.Sigmoid, bias=bg_c[:, hc:hc + 1],
            scale=1.0), extra_waits=list(g_unit.rs))
        g_unit.rs = []
        g_unit.w = ("s", ge)

        he = pr.emit("scalar", lambda e, hc=hc: e.activation(
            h_f[:, :], u_f[:, :], AF.Identity, bias=cb_c[:, hc:hc + 1],
            scale=cw_c[:, hc * 3 + 1:hc * 3 + 2]),
            waits=[u_unit.w] + list(h_unit.rs), inc="s")
        h_unit.rs = []
        vc = pr.emit("vector", lambda e, hc=hc: e.scalar_tensor_tensor(
            h_f[:, 1:L], u_f[:, 0:L - 1], cw_c[:, hc * 3:hc * 3 + 1],
            h_f[:, 1:L], ALU.mult, ALU.add), waits=[("s", he)], inc="v")
        vc = pr.emit("vector", lambda e, hc=hc: e.scalar_tensor_tensor(
            h_f[:, 0:L - 1], u_f[:, 1:L], cw_c[:, hc * 3 + 2:hc * 3 + 3],
            h_f[:, 0:L - 1], ALU.mult, ALU.add), inc="v")
        vg = pr.emit("vector", lambda e: e.tensor_sub(
            u_f[:, :], u_f[:, :], h_f[:, :]),
            waits=[("v", vc), g_unit.w], inc="v")
        vg = pr.emit("vector", lambda e: e.tensor_tensor(
            u_f[:, :], u_f[:, :], g_f[:, :], ALU.mult), inc="v")
        vg = pr.emit("vector", lambda e: e.tensor_add(
            u_f[:, :], u_f[:, :], h_f[:, :]), inc="v")
        g_unit.rs.append(("v", vg))
        h_unit.rs.append(("v", vg))
        if debug and hc == 0:
            dma_s(lambda e: e.dma_start(out=dbg["dbg_ug"], in_=u_f[:, :]),
                  waits=[("v", vg)])

        for s_i in range(S):
            col = hc * S + s_i
            pr.emit("vector", lambda e, col=col: e.tensor_tensor_scan(
                st_f[:, :], ad_c[:, col:col + 1].broadcast_to((P, L)),
                u_f[:, :], 0.0, ALU.mult, ALU.add), inc="v")
            if s_i == 0:
                vy = pr.emit("vector", lambda e, col=col: e.tensor_scalar(
                    ya_f[:, :], st_f[:, :], bf_c[:, col:col + 1], None,
                    ALU.mult), inc="v")
            elif s_i < S - 1:
                vy = pr.emit("vector", lambda e, col=col:
                             e.scalar_tensor_tensor(
                                 ya_f[:, :], st_f[:, :], bf_c[:, col:col + 1],
                                 ya_f[:, :], ALU.mult, ALU.add), inc="v")
            else:
                vy = pr.emit("vector", lambda e, col=col, hc=hc:
                             e.scalar_tensor_tensor(
                                 ybf[:, hc * L:(hc + 1) * L], st_f[:, :],
                                 bf_c[:, col:col + 1], ya_f[:, :],
                                 ALU.mult, ALU.add), inc="v")
        u_unit.rs.append(("v", vy))
        if debug and hc == 0:
            dy = pr.emit("vector", lambda e: e.tensor_copy(
                ya_f[:, :], ybf[:, 0:L]), waits=[("v", vy)], inc="v")
            dma_s(lambda e: e.dma_start(out=dbg["dbg_y"], in_=ya_f[:, :]),
                  waits=[("v", dy)])
    ybf_done = ("v", vy)
    if debug:
        dma_g(lambda e: e.dma_start(out=dbg["dbg_ybf"], in_=ybf[:, :]),
              waits=[ybf_done])
    p2a.close()

    p2b = ExitStack()
    def sb2b(name, shape, dtype):
        return p2b.enter_context(nc.sbuf_tensor(name, list(shape), dtype))
    sfull_b = sb2b("sfull_b", (P, NDC * L), BF16)
    xo_f = sb2b("xo_f", (P, L), F32)
    wout0 = sb2b("wout0", (P, H), BF16)
    stage2A_b = sb2b("stage2A_b", (P, L), BF16)
    stage2B_b = sb2b("stage2B_b", (P, L), BF16)
    srow2_m = sb2b("srow2_m", (1, L), F32)
    rbc_f = sb2b("rbc_f", (P, L), F32)
    srow2_t = rbc_f

    bar_p4 = eng_bar()
    ms_rowp2 = pr.emit("gpsimd", lambda e: e.memset(stage2A_b[:, :], 0.0),
                       waits=bar_p4, inc="g")
    ms_rowp2 = pr.emit("gpsimd", lambda e: e.memset(stage2B_b[:, :], 0.0),
                       inc="g")
    _wu = Unit(wout0)
    wout_units = [_wu, _wu]
    xo_unit = Unit(xo_f)
    sq2_b = stage2B_b
    u_sq2 = Unit(sq2_b)

    def load_wout(dc, buf_i, bar=()):
        u = wout_units[buf_i]
        d2 = dma_g(lambda e, dc=dc, b2=u.ap: e.dma_start(
            out=b2[:, :], in_=dr["wout_t"][dc]), waits=list(u.rs) + list(bar))
        u.rs = []
        u.w = ("qg", d2)

    load_wout(0, 0, bar=bar_p4)

    ra = pr.emit("vector", lambda e: e.tensor_copy(
        stage2A_b[0:1, :], rows_own[0:1, 0:L]),
        waits=[("g", ms_rowp2)], inc="v")
    rb = pr.emit("vector", lambda e: e.tensor_copy(
        stage2B_b[0:1, :], rows_own[0:1, L:2 * L]), inc="v")

    for dc in range(NDC):
        wout_u = wout_units[dc % 2]
        xod = dma_s(lambda e, dc=dc: e.dma_start(
            out=xo_f[:, :], in_=dr["x_own"][dc * P:(dc + 1) * P, :]),
            waits=list(xo_unit.rs) + (bar_p4 if dc == 0 else []))
        xo_unit.rs = []
        bcast_rows(uB, selb[:, 0:P], stage2A_b, waits=[("v", rb)])
        unit_read("vector", uB, lambda e: e.tensor_tensor(
            xo_f[:, :], xo_f[:, :], psB[:, :], ALU.mult),
            extra_waits=[("qs", xod)])
        bcast_rows(uB, selb[:, 0:P], stage2B_b)
        va = unit_read("vector", uB, lambda e: e.tensor_tensor(
            xo_f[:, :], xo_f[:, :], psB[:, :], ALU.add))

        if debug and dc == 0:
            dma_s(lambda e: e.dma_start(out=dbg["dbg_xo"], in_=xo_f[:, :]),
                  waits=[("v", va)])
        fns = []
        for ts in range(4):
            for hc in range(NHC):
                fns.append(lambda e, ts=ts, hc=hc, wb=wout_u.ap: e.matmul(
                    psA[:, ts * 512:(ts + 1) * 512],
                    wb[:, hc * P:(hc + 1) * P],
                    ybf[:, hc * L + ts * 512: hc * L + (ts + 1) * 512],
                    start=(hc == 0), stop=(hc == NHC - 1)))
        tw = unit_write(uA, fns, waits=[wout_u.w, ybf_done])
        if debug and dc == 0:
            vao = unit_read("vector", uA, lambda e: e.tensor_copy(
                srow2_t[0:1, :], psA[0:1, :]))
            dma_s(lambda e: e.dma_start(out=dbg["dbg_ao"][0:1, :],
                                        in_=srow2_t[0:1, :]),
                  waits=[("v", vao)])
        wout_u.rs.append(("t", tw))
        if dc + 1 < NDC:
            load_wout(dc + 1, 0)
        se = unit_read("vector", uA, lambda e, dc=dc: e.scalar_tensor_tensor(
            sfull_b[:, dc * L:(dc + 1) * L], psA[:, :],
            bout_c[:, dc:dc + 1], xo_f[:, :], ALU.add, ALU.add),
            extra_waits=[("v", va), ("qs", consts_qs)])
        xo_unit.rs.append(("v", se))

    if debug:
        dma_g(lambda e: e.dma_start(out=dbg["dbg_sfull"],
                                    in_=sfull_b[:, 0:L]),
              waits=[("v", pr.cnt["v"])])
    last_mm = None
    first_sum = True
    first_sq = True
    for dc in range(NDC):
        for ts in range(4):
            w2 = []
            if ts == 0 and first_sum:
                w2 = [("v", pr.cnt["v"])] + list(uA.rs)
                uA.rs = []
                first_sum = False
            last_sum = pr.emit(
                "tensor", lambda e, ts=ts, dc=dc: e.matmul(
                    psA[0:1, ts * 512:(ts + 1) * 512], onescol[:, :],
                    sfull_b[:, dc * L + ts * 512: dc * L + (ts + 1) * 512],
                    start=(dc == 0), stop=(dc == NDC - 1)),
                waits=w2, inc="t" if ts == 3 else None)
        sq2 = pr.emit("scalar", lambda e, dc=dc: e.activation(
            sq2_b[:, :], sfull_b[:, dc * L:(dc + 1) * L], AF.Square,
            bias=0.0, scale=1.0),
            waits=list(u_sq2.rs)
            + ([("v", pr.cnt["v"]), ("t", pr.cnt["t"])] if dc == 0 else []),
            inc="s")
        u_sq2.rs = []
        for ts in range(4):
            w2 = []
            if ts == 0:
                w2.append(("s", sq2))
                if first_sq:
                    w2.extend(uB.rs)
                    uB.rs = []
                    first_sq = False
            last_mm = pr.emit(
                "tensor", lambda e, ts=ts, dc=dc: e.matmul(
                    psB[0:1, ts * 512:(ts + 1) * 512], onescol[:, :],
                    sq2_b[:, ts * 512:(ts + 1) * 512],
                    start=(dc == 0), stop=(dc == NDC - 1)),
                waits=w2, inc="t" if ts == 3 else None)
        u_sq2.rs.append(("t", last_mm))
    uA.w = ("t", last_sum)
    uA.rs = []
    uB.w = ("t", last_mm)
    uB.rs = []

    v6 = ln_rows(srow2_m[0:1, :], srow2_t[0:1, :], uA, uB, want_b=False)
    if debug:
        dma_s(lambda e: e.dma_start(out=dbg["dbg_m2"], in_=srow2_m[0:1, :]),
              waits=[("v", v6)])
        vtmp = pr.emit("vector", lambda e: e.tensor_copy(
            srow2_t[0:1, :], psA[0:1, :]), waits=[("v", v6)], inc="v")
        dma_s(lambda e: e.dma_start(out=dbg["dbg_r2"], in_=srow2_t[0:1, :]),
              waits=[("v", vtmp)])
    vr = pr.emit("vector", lambda e: e.tensor_copy(
        stage2A_b[0:1, :], srow2_m[0:1, :]), waits=[("v", v6)], inc="v")
    vr = pr.emit("vector", lambda e: e.tensor_copy(
        stage2B_b[0:1, :], psA[0:1, :]), inc="v")
    bcast_rows(uA, selb[:, 0:P], stage2A_b, waits=[("v", vr)])
    mbc_b = wout0
    em = unit_read("scalar", uA, lambda e: e.activation(
        mbc_b[:, :], psA[:, :], AF.Copy, bias=0.0, scale=1.0))
    bcast_rows(uB, selb[:, 0:P], stage2B_b, waits=[("v", vr)])
    erb = unit_read("scalar", uB, lambda e: e.activation(
        rbc_f[:, :], psB[:, :], AF.Copy, bias=0.0, scale=1.0))
    sdone = None
    vb = None
    for dc in range(NDC):
        va = pr.emit("vector", lambda e, dc=dc: e.tensor_sub(
            xo_f[:, :], sfull_b[:, dc * L:(dc + 1) * L], mbc_b[:, :]),
            waits=[("s", em), uB.w]
            + (list(xo_unit.rs) if dc == 0 else [("s", sdone)]), inc="v")
        vb = pr.emit("vector", lambda e: e.tensor_tensor(
            xo_f[:, :], xo_f[:, :], rbc_f[:, :], ALU.mult),
            waits=[("s", erb)], inc="v")
        if debug and dc == NDC - 1:
            vdb = pr.emit("vector", lambda e: e.tensor_copy(
                sq2_b[:, :], psB[:, :]), inc="v")
            dgx = dma_s(lambda e: e.dma_start(out=dbg["dbg_xo7"],
                                              in_=xo_f[:, :]),
                        waits=[("v", vb)])
            dgr = dma_g(lambda e: e.dma_start(out=dbg["dbg_rbc"],
                                              in_=sq2_b[:, :]),
                        waits=[("v", vdb)])
            xo_unit.rs.append(("qs", dgx))
            uB.rs.append(("qg", dgr))
            dgm = dma_g(lambda e: e.dma_start(out=dbg["dbg_mbc"],
                                              in_=mbc_b[:, :]),
                        waits=[("s", em)])
        sdone = pr.emit("scalar", lambda e, dc=dc: e.activation(
            sbf[:, dc * L:(dc + 1) * L], xo_f[:, :], AF.Identity,
            bias=lnb_c[:, dc:dc + 1], scale=lng_c[:, dc:dc + 1]),
            waits=[("v", vb)], inc="s")
    uB.rs.append(("v", vb))
    s_done = ("s", sdone)
    if debug:
        dma_g(lambda e: e.dma_start(out=dbg["dbg_sfull2"], in_=sfull_b[:, :]),
              waits=[s_done])
    if debug:
        dma_g(lambda e: e.dma_start(out=dbg["dbg_s"], in_=sbf[:, 0:L]),
              waits=[s_done])
    p2b.close()
    p2.close()

    p3s = ExitStack()
    def sb3(name, shape, dtype):
        return p3s.enter_context(nc.sbuf_tensor(name, list(shape), dtype))
    act_b = sb3("act_b", (P, NIC * HALF), BF16)
    acc_f = sb3("acc_f", (P, NDC * HALF), F32)
    sel8b = sb3("sel8b", (P, E * P), BF16)
    wgu0 = sb3("wgu0", (P, 2 * D), BF16)
    wgu1 = sb3("wgu1", (P, 2 * D), BF16)
    wdd0 = sb3("wdd0", (P, I), BF16)
    wdd1 = sb3("wdd1", (P, I), BF16)
    lgt_f = sb3("lgt_f", (P, P), F32)
    mxall_f = sb3("mxall_f", (P, 16 * 8), F32)
    z8_f = sb3("z8_f", (P, 8), F32)
    o8_f = sb3("o8_f", (P, 8), F32)
    e1all_f = sb3("e1all_f", (P, P), F32)
    cstg_b = sb3("cstg_b", (P, P), BF16)
    l2all_f = sb3("l2all_f", (P, P), F32)
    m12 = sb3("m12", (P, 4 * 16), F32)
    cpad_b = sb3("cpad_b", (P, L), BF16)
    cbc_b = sb3("cbc_b", (P, HALF), BF16)
    g_sb = sb3("g_sb", (P, HALF), BF16)
    u_sb = sb3("u_sb", (P, HALF), BF16)
    t_sb = sb3("t_sb", (P, HALF), BF16)
    xo2_f = sb3("xo2_f", (P, HALF), F32)
    gs_b = sb3("gs_b", (P, HALF), BF16)
    pr_f = sb3("pr_f", (P, HALF), F32)

    bar_p5 = eng_bar()
    for n2 in range(E):
        ld_sel8 = dma_g(lambda e, n2=n2: e.dma_start(
            out=sel8b[:, n2 * P:(n2 + 1) * P], in_=dr["sel8"][n2]),
            waits=bar_p5 if n2 == 0 else ())
    ms_cpad = pr.emit("gpsimd", lambda e: e.memset(cpad_b[:, :], 0.0),
                      waits=bar_p5, inc="g")

    fns = []
    for tb in range(16):
        for dc in range(NDC):
            fns.append(lambda e, tb=tb, dc=dc: e.matmul(
                psA[:, tb * 8:(tb + 1) * 8],
                xnb[:, dc * L + tb * P: dc * L + (tb + 1) * P],
                gwb[:, dc * E:(dc + 1) * E],
                start=(dc == 0), stop=(dc == NDC - 1)))
    unit_write(uA, fns, waits=[("qg", ld_gw)])
    unit_read("vector", uA, lambda e: e.tensor_copy(
        lgt_f[:, :], psA[:, 0:P]), extra_waits=bar_p5)
    if debug:
        dma_s(lambda e: e.dma_start(out=dbg["dbg_lgt"], in_=lgt_f[:, :]),
              waits=[("v", pr.cnt["v"])])
    vm = None
    for tb in range(16):
        sl = lgt_f[:, tb * 8:(tb + 1) * 8]
        vm = pr.emit("vector", lambda e, sl=sl, tb=tb: e.max(
            mxall_f[:, tb * 8:(tb + 1) * 8], sl), inc="v")
    ms_z8 = pr.emit("vector", lambda e: e.memset(z8_f[:, :], 0.0), inc="v")
    ms_o8 = pr.emit("vector", lambda e: e.memset(o8_f[:, :], 1.0), inc="v")
    vp = pr.emit("vector", lambda e: e.tensor_sub(
        m12[:, 0:16], mxall_f[:, 0:128:8], mxall_f[:, 1:128:8]),
        waits=[("v", vm)], inc="v")
    pr.emit("scalar", lambda e: e.activation(
        m12[:, 32:48], m12[:, 0:16], AF.Sigmoid, bias=0.0, scale=1.0),
        waits=[("v", vp)], inc="s")
    vp2s = pr.emit("scalar", lambda e: e.activation(
        m12[:, 48:64], m12[:, 0:16], AF.Sigmoid, bias=0.0, scale=-1.0),
        inc="s")
    vp2 = pr.emit("vector", lambda e: e.tensor_copy(
        lgt_f[0:1, 0:1], lgt_f[0:1, 0:1]), waits=[("s", vp2s)], inc="v")
    if debug:
        dma_s(lambda e: e.dma_start(out=dbg["dbg_m12"], in_=m12[:, :]),
              waits=[("v", vp2)])
    for tb in range(16):
        pr.emit("vector", lambda e, tb=tb: e.scalar_tensor_tensor(
            e1all_f[:, tb * 8:(tb + 1) * 8], o8_f[:, :],
            mxall_f[:, tb * 8:tb * 8 + 1], z8_f[:, :], ALU.mult, ALU.add),
            waits=[("v", ms_o8), ("s", vp2s)] if tb == 0 else [])
    for tb in range(16):
        pr.emit("vector", lambda e, tb=tb: e.scalar_tensor_tensor(
            l2all_f[:, tb * 8:(tb + 1) * 8], o8_f[:, :],
            mxall_f[:, tb * 8 + 1:tb * 8 + 2], z8_f[:, :],
            ALU.mult, ALU.add))
    pr.emit("vector", lambda e: e.tensor_sub(
        e1all_f[:, :], e1all_f[:, :], lgt_f[:, :]))
    pr.emit("vector", lambda e: e.tensor_sub(
        l2all_f[:, :], l2all_f[:, :], lgt_f[:, :]))
    pr.emit("vector", lambda e: e.tensor_scalar_mul(
        e1all_f[:, :], e1all_f[:, :], -1e30))
    pr.emit("vector", lambda e: e.tensor_scalar_mul(
        l2all_f[:, :], l2all_f[:, :], -1e30))
    pr.emit("vector", lambda e: e.tensor_scalar_add(
        e1all_f[:, :], e1all_f[:, :], 1.0))
    pr.emit("vector", lambda e: e.tensor_scalar_add(
        l2all_f[:, :], l2all_f[:, :], 1.0))
    pr.emit("vector", lambda e: e.tensor_scalar_max(
        e1all_f[:, :], e1all_f[:, :], 0.0))
    pr.emit("vector", lambda e: e.tensor_scalar_max(
        l2all_f[:, :], l2all_f[:, :], 0.0))
    pr.emit("vector", lambda e: e.tensor_scalar_min(
        l2all_f[:, :], l2all_f[:, :], 1.0))
    pr.emit("vector", lambda e: e.tensor_sub(
        l2all_f[:, :], l2all_f[:, :], e1all_f[:, :]))
    for tb in range(16):
        pr.emit("vector", lambda e, tb=tb: e.scalar_tensor_tensor(
            e1all_f[:, tb * 8:(tb + 1) * 8],
            e1all_f[:, tb * 8:(tb + 1) * 8],
            m12[:, 32 + tb:32 + tb + 1], z8_f[:, :], ALU.mult, ALU.add))
    for tb in range(16):
        pr.emit("vector", lambda e, tb=tb: e.scalar_tensor_tensor(
            e1all_f[:, tb * 8:(tb + 1) * 8],
            l2all_f[:, tb * 8:(tb + 1) * 8],
            m12[:, 48 + tb:48 + tb + 1],
            e1all_f[:, tb * 8:(tb + 1) * 8], ALU.mult, ALU.add))
    vcc = pr.emit("vector", lambda e: e.tensor_copy(
        cstg_b[:, :], e1all_f[:, :]), inc="v")
    tpos_last = None
    for tb in range(16):
        tpos_last = pr.emit("tensor", lambda e, tb=tb: e.matmul(
            psB[0:8, tb * P:(tb + 1) * P],
            cstg_b[:, tb * 8:(tb + 1) * 8],
            idb[:, :], start=True, stop=True),
            waits=[("v", vcc), ("qg", ld_id)]
            + (list(uB.rs) if tb == 0 else []),
            inc="t")
    uA.rs.append(("v", pr.cnt["v"]))
    uB.w = ("t", tpos_last)
    uB.rs = []
    vcp = unit_read("vector", uB, lambda e: e.tensor_copy(
        cpad_b[0:8, :], psB[0:8, :]), extra_waits=[("g", ms_cpad)])
    if debug:
        dgc = dma_g(lambda e: e.dma_start(out=dbg["dbg_comb"],
                                          in_=cpad_b[:, :]),
                    waits=[("v", vcp)])
        uB.rs.append(("qg", dgc))

    wgu_units = [Unit(wgu0), Unit(wgu1)]
    wdd_units = [Unit(wdd0), Unit(wdd1)]
    u_gsb = Unit(g_sb)
    u_usb = Unit(u_sb)
    u_tsb = Unit(t_sb)
    u_cbc = Unit(cbc_b)
    u_acc = Unit(acc_f)
    u_xo2 = Unit(xo2_f)
    u_gs = Unit(gs_b)
    u_pr = Unit(pr_f)
    out_dmas = []

    first_wgu = [True]

    def load_wgu(src1, src2, idx, buf_i):
        u = wgu_units[buf_i]
        bar = bar_p5 if first_wgu[0] else ()
        first_wgu[0] = False
        dma_g(lambda e, b2=u.ap, s2=src1, idx=idx: e.dma_start(
            out=b2[:, 0:D], in_=s2[idx]), waits=list(u.rs) + list(bar))
        d2 = dma_g(lambda e, b2=u.ap, s2=src2, idx=idx: e.dma_start(
            out=b2[:, D:2 * D], in_=s2[idx]))
        u.rs = []
        u.w = ("qg", d2)

    first_wdd = [True]

    def load_wdd(src, idx, width, buf_i):
        u = wdd_units[buf_i]
        bar = bar_p5 if first_wdd[0] else ()
        first_wdd[0] = False
        d2 = dma_g(lambda e, b2=u.ap, s2=src, idx=idx, w2=width: e.dma_start(
            out=b2[:, 0:w2], in_=s2[idx]), waits=list(u.rs) + list(bar))
        u.rs = []
        u.w = ("qg", d2)

    for hf in range(2):
        t0 = hf * HALF
        nseg = 0

        for ei in range(E + 1):
            shared = ei == E
            nic = NSC if shared else NIC
            if shared:
                pass
            if not shared:
                fns = [lambda e, ts=ts, ei=ei, t0=t0: e.matmul(
                    psB[:, ts * 512:(ts + 1) * 512],
                    sel8b[:, ei * P:(ei + 1) * P],
                    cpad_b[:, t0 + ts * 512: t0 + (ts + 1) * 512],
                    start=True, stop=True) for ts in range(2)]
                unit_write(uB, fns, waits=[("v", vcp), ("qg", ld_sel8)])
                unit_read("scalar", uB, lambda e: e.activation(
                    cbc_b[:, :], psB[:, 0:HALF], AF.Copy, bias=0.0,
                    scale=1.0), extra_waits=list(u_cbc.rs))
                u_cbc.rs = []
                u_cbc.w = uB.rs[-1]

            vact = None
            for ic in range(nic):
                wgu_u = wgu_units[ic % 2]
                if ic == 0:
                    if shared:
                        load_wgu(dr["wsg_t"], dr["wsu_t"], 0, 0)
                        load_wgu(dr["wsg_t"], dr["wsu_t"], 1, 1)
                    else:
                        load_wgu(dr["wgexp_t"], dr["wuexp_t"], (ei, 0), 0)
                        load_wgu(dr["wgexp_t"], dr["wuexp_t"], (ei, 1), 1)

                def mk_fmm(ps, woff, ts, dc, wgu_u=wgu_u, t0=t0):
                    return lambda e: e.matmul(
                        ps[:, ts * 512:(ts + 1) * 512],
                        wgu_u.ap[:, woff + dc * P: woff + (dc + 1) * P],
                        xnb[:, dc * L + t0 + ts * 512:
                            dc * L + t0 + (ts + 1) * 512],
                        start=(dc == 0), stop=(dc == NDC - 1))
                fns = [mk_fmm(psA, 0, ts, dc)
                       for ts in range(2) for dc in range(NDC)]
                tw = unit_write(uA, fns, waits=[wgu_u.w])
                wgu_u.rs.append(("t", tw))
                fns = [mk_fmm(psB, D, ts, dc)
                       for ts in range(2) for dc in range(NDC)]
                tw = unit_write(uB, fns, waits=[wgu_u.w])
                wgu_u.rs.append(("t", tw))
                if ic + 2 < nic:
                    if shared:
                        load_wgu(dr["wsg_t"], dr["wsu_t"], ic + 2, ic % 2)
                    else:
                        load_wgu(dr["wgexp_t"], dr["wuexp_t"],
                                 (ei, ic + 2), ic % 2)
                sg = unit_read("scalar", uA, lambda e: e.activation(
                    g_sb[:, :], psA[:, 0:HALF], AF.Silu, bias=0.0,
                    scale=1.0), extra_waits=list(u_gsb.rs))
                u_gsb.rs = []
                u_gsb.w = ("s", sg)
                su = unit_read("scalar", uB, lambda e: e.activation(
                    u_sb[:, :], psB[:, 0:HALF], AF.Copy, bias=0.0,
                    scale=1.0), extra_waits=list(u_usb.rs))
                u_usb.rs = []
                u_usb.w = ("s", su)
                vt = pr.emit("vector", lambda e: e.tensor_tensor(
                    t_sb[:, :], g_sb[:, :], u_sb[:, :], ALU.mult),
                    waits=[u_gsb.w, u_usb.w] + list(u_tsb.rs), inc="v")
                u_tsb.rs = []
                u_gsb.rs.append(("v", vt))
                u_usb.rs.append(("v", vt))
                if shared:
                    vact = pr.emit("vector", lambda e, ic=ic: e.tensor_copy(
                        act_b[:, ic * HALF:(ic + 1) * HALF], t_sb[:, :]),
                        inc="v")
                else:
                    vact = pr.emit("vector", lambda e, ic=ic: e.tensor_tensor(
                        act_b[:, ic * HALF:(ic + 1) * HALF], t_sb[:, :],
                        cbc_b[:, :], ALU.mult), waits=[u_cbc.w], inc="v")
                u_tsb.rs.append(("v", vact))
                if not shared:
                    u_cbc.rs.append(("v", vact))
                if debug and hf == 0 and shared and ic == 0:
                    d1 = dma_g(lambda e: e.dma_start(
                        out=dbg["dbg_gsb"], in_=g_sb[:, :]),
                        waits=[("v", vt)])
                    d2 = dma_g(lambda e: e.dma_start(
                        out=dbg["dbg_usb"], in_=u_sb[:, :]),
                        waits=[("v", vt)])
                    d3 = dma_g(lambda e: e.dma_start(
                        out=dbg["dbg_act"], in_=act_b[:, 0:HALF]),
                        waits=[("v", vact)])
                    u_gsb.rs.append(("qg", d1))
                    u_usb.rs.append(("qg", d2))
                    u_tsb.rs.append(("qg", d3))

            for dc in range(NDC):
                wdd_u = wdd_units[dc % 2]
                if dc == 0:
                    if shared:
                        load_wdd(dr["wsd_t"], 0, ISH, 0)
                        load_wdd(dr["wsd_t"], 1, ISH, 1)
                    else:
                        load_wdd(dr["wdexp_t"], (ei, 0), I, 0)
                        load_wdd(dr["wdexp_t"], (ei, 1), I, 1)
                fns = []
                for ts in range(2):
                    for ic in range(nic):
                        fns.append(lambda e, ts=ts, ic=ic, wdd_u=wdd_u,
                                   nic=nic: e.matmul(
                                       psA[:, ts * 512:(ts + 1) * 512],
                                       wdd_u.ap[:, ic * P:(ic + 1) * P],
                                       act_b[:, ic * HALF + ts * 512:
                                             ic * HALF + (ts + 1) * 512],
                                       start=(ic == 0), stop=(ic == nic - 1)))
                tw = unit_write(uA, fns, waits=[wdd_u.w, ("v", vact)])
                wdd_u.rs.append(("t", tw))
                if dc + 2 < NDC:
                    if shared:
                        load_wdd(dr["wsd_t"], dc + 2, ISH, dc % 2)
                    else:
                        load_wdd(dr["wdexp_t"], (ei, dc + 2), I, dc % 2)
                if ei == 0:
                    unit_read("vector", uA, lambda e, dc=dc: e.tensor_copy(
                        acc_f[:, dc * HALF:(dc + 1) * HALF], psA[:, 0:HALF]),
                        extra_waits=list(u_acc.rs))
                    if dc == NDC - 1:
                        u_acc.rs = []
                else:
                    unit_read("vector", uA, lambda e, dc=dc: e.tensor_tensor(
                        acc_f[:, dc * HALF:(dc + 1) * HALF],
                        acc_f[:, dc * HALF:(dc + 1) * HALF],
                        psA[:, 0:HALF], ALU.add))
        acc_done = uA.rs[-1]
        if debug and hf == 0:
            dacc = dma_s(lambda e: e.dma_start(out=dbg["dbg_accf"],
                                               in_=acc_f[:, :]),
                         waits=[acc_done])
            u_acc.rs.append(("qs", dacc))
        if debug and hf == 0:
            va2 = pr.emit("vector", lambda e: e.tensor_copy(
                pr_f[:, :], acc_f[:, 0:HALF]), waits=[acc_done], inc="v")
            dma_s(lambda e: e.dma_start(
                out=dbg["dbg_acc"].rearrange("p (a c) -> p a c", a=2)[:, 0, :]
                if False else dbg["dbg_acc"][:, 0:HALF], in_=pr_f[:, :]),
                waits=[("v", va2)])

        for dc in range(NDC):
            xod = dma_s(lambda e, dc=dc, t0=t0: e.dma_start(
                out=xo2_f[:, :],
                in_=dr["x_own"][dc * P:(dc + 1) * P, t0:t0 + HALF]),
                waits=list(u_xo2.rs)
                + (bar_p5 if (hf == 0 and dc == 0) else []))
            u_xo2.rs = []
            sgs = pr.emit("scalar", lambda e, dc=dc: e.activation(
                gs_b[:, :], acc_f[:, dc * HALF:(dc + 1) * HALF], AF.Silu,
                bias=0.0, scale=1.0),
                waits=[acc_done] + list(u_gs.rs), inc="s")
            u_gs.rs = []
            u_acc.rs.append(("s", sgs))
            vpr = pr.emit("vector", lambda e, dc=dc, t0=t0: e.tensor_tensor(
                pr_f[:, :], sbf[:, dc * L + t0: dc * L + t0 + HALF],
                gs_b[:, :], ALU.mult),
                waits=[("s", sgs), s_done] + list(u_pr.rs), inc="v")
            u_gs.rs.append(("v", vpr))
            vpr = pr.emit("vector", lambda e: e.tensor_tensor(
                pr_f[:, :], pr_f[:, :], xo2_f[:, :], ALU.add),
                waits=[("qs", xod)], inc="v")
            u_xo2.rs.append(("v", vpr))
            u_acc.rs.append(("v", vpr))
            od = dma_s(lambda e, dc=dc, t0=t0: e.dma_start(
                out=y_out[dc * P:(dc + 1) * P, t0:t0 + HALF],
                in_=pr_f[:, :]), waits=[("v", vpr)])
            out_dmas.append(od)
            u_pr.rs = [("qs", od)]

    if debug or sdump:
        dma_g(lambda e: e.dma_start(out=dbg["dbg_sfin"], in_=sbf[:, :]),
              waits=[("qs", out_dmas[-1])])
    pr.emit("sync", lambda e: e.nop(),
            waits=[("qs", out_dmas[-1]),
                   ("qg", pr.cnt.get("qg", 0))])
    p3s.close()

    with nc.Block() as blk:
        sems = {}
        sem_ctx = ExitStack()
        for name in ("qs", "qg", "t", "v", "s", "g"):
            sems[name] = sem_ctx.enter_context(nc.semaphore(name))

        @blk.sync
        def _(eng):
            pr.replay("sync", eng, sems)

        @blk.gpsimd
        def _(eng):
            pr.replay("gpsimd", eng, sems)

        @blk.vector
        def _(eng):
            pr.replay("vector", eng, sems)

        @blk.scalar
        def _(eng):
            pr.replay("scalar", eng, sems)

        @blk.tensor
        def _(eng):
            pr.replay("tensor", eng, sems)

        sem_ctx.close()
    ctx.close()
    return nc


_NC_CACHE = {}


def kernel(**inputs):
    inputs = {k: np.asarray(v, dtype=np.float32) for k, v in inputs.items()}
    g = host_prep(inputs)
    key = ("v1", DEBUG)
    if key not in _NC_CACHE:
        _NC_CACHE[key] = build_nc(debug=DEBUG)
    nc = _NC_CACHE[key]
    base = {k: v for k, v in g.items()}
    in_maps = []
    for c in range(8):
        m = dict(base)
        m["x_all"] = np.ascontiguousarray(
            np.concatenate([g["x_all"][c:], g["x_all"][:c]], axis=0))
        m["x_own"] = np.ascontiguousarray(g["x_all"][c])
        in_maps.append(m)
    res = run_bass_kernel_spmd(nc, in_maps, core_ids=list(range(8)))
    outs = [res.results[c]["y_fm"] for c in range(8)]
    out = np.stack(outs, axis=0).transpose(0, 2, 1)
    return np.ascontiguousarray(out)



